# version 18 (final)
# baseline (speedup 1.0000x reference)
import contextlib
from collections import defaultdict

import numpy as np
import concourse.bass as bass
import concourse.mybir as mybir
from concourse.bass_utils import run_bass_kernel_spmd

F32 = mybir.dt.float32
BF16 = mybir.dt.bfloat16
AF = mybir.ActivationFunctionType
ALU = mybir.AluOpType
AX = mybir.AxisListType

T = 4096
C = 1024
NCH = 8
NQB = 4
QBW = 512
NS = 32
BW = 32
NKT = 32
EPS = 1e-5

ENGINES = ("tensor", "scalar", "vector", "gpsimd", "sync")
TAIL_LIM = 99
TAIL_SUB = 'all'
DEBUG_DUMP = None
DEBUG_HP = 0
ATT_BARRIER = False


class Sch:

    def __init__(self):
        self.q = {e: [] for e in ENGINES}
        self.hist = {e: [] for e in ENGINES}
        self.n = defaultdict(int)

    def op(self, eng, fn=None, waits=(), inc=None, amt=1):
        waits = tuple((s, v) for (s, v) in waits if v and v > 0)
        tok = None
        if inc is not None:
            self.n[inc] += amt
            tok = self.n[inc]
        rec = (fn, waits, inc, amt)
        self.q[eng].append(rec)
        self.hist[eng].append(rec)
        return tok

    def barrier(self):
        snap = [(s, v) for s, v in self.n.items() if v > 0]
        for e in ENGINES:
            self.op(e, None, waits=snap)

    def flush(self, nc, sems):
        with nc.Block() as blk:
            for e in ENGINES:
                ops = self.q[e]
                if not ops:
                    continue

                def body(eng, ops=ops):
                    for fn, waits, inc, amt in ops:
                        for s, v in waits:
                            eng.wait_ge(sems[s], v)
                        if fn is not None:
                            inst = fn(eng)
                            if inc is not None:
                                inst.then_inc(sems[inc], amt)

                getattr(blk, e)(body)
        self.q = {e: [] for e in ENGINES}

    def verify(self):
        ptr = {e: 0 for e in ENGINES}
        val = defaultdict(int)
        total = sum(len(self.hist[e]) for e in ENGINES)
        done = 0
        progress = True
        while progress:
            progress = False
            for e in ENGINES:
                while ptr[e] < len(self.hist[e]):
                    fn, waits, inc, amt = self.hist[e][ptr[e]]
                    if any(val[s] < v for s, v in waits):
                        break
                    if inc is not None:
                        val[inc] += amt
                    ptr[e] += 1
                    done += 1
                    progress = True
        if done != total:
            stuck = {e: (ptr[e], len(self.hist[e]),
                         self.hist[e][ptr[e]][1] if ptr[e] < len(self.hist[e]) else None)
                     for e in ENGINES}
            raise RuntimeError(f"schedule deadlock: sems={dict(val)} stuck={stuck}")


def build_program(stop_after=None):
    _stages = ["p1", "knorm", "bnd_q", "bnd_norm", "bnd_attn", "bnd_tail",
               "qb0_q", "qb0_norm", "qb0_attn", "qb0_tail",
               "qb1", "qb2", "qb3", "p3_w", "p3_m0", "p3_m1", "p3_m2", "p3_m3"]
    _lim = len(_stages) if stop_after is None else _stages.index(stop_after) + 1
    _ctr = [0]

    def go(name):
        assert _stages[_ctr[0]] == name, (name, _ctr[0])
        _ctr[0] += 1
        return _ctr[0] <= _lim

    nc = bass.Bass("TRN2", target_bir_lowering=False, debug=False)

    xT = nc.dram_tensor("xT", [C, T], F32, kind="ExternalInput").ap()
    xTq = nc.dram_tensor("xTq", [C, 2048], F32, kind="ExternalInput").ap()
    xTb = nc.dram_tensor("xTb", [C, BW], F32, kind="ExternalInput").ap()
    wA = nc.dram_tensor("wA", [C, 3 * C], F32, kind="ExternalInput").ap()
    wP = nc.dram_tensor("wP", [C, C], F32, kind="ExternalInput").ap()
    wRT = nc.dram_tensor("wRT", [C, C], F32, kind="ExternalInput").ap()
    wKT = nc.dram_tensor("wKT", [C, C], F32, kind="ExternalInput").ap()
    wVT = nc.dram_tensor("wVT", [C, C], F32, kind="ExternalInput").ap()
    bqc = nc.dram_tensor("bqc", [128, 8], F32, kind="ExternalInput").ap()
    bkc = nc.dram_tensor("bkc", [128, 8], F32, kind="ExternalInput").ap()
    bvc = nc.dram_tensor("bvc", [128, 8], F32, kind="ExternalInput").ap()
    bpc = nc.dram_tensor("bpc", [128, 8], F32, kind="ExternalInput").ap()
    murc = nc.dram_tensor("murc", [128, 8], F32, kind="ExternalInput").ap()
    mukc = nc.dram_tensor("mukc", [128, 8], F32, kind="ExternalInput").ap()
    maskd = nc.dram_tensor("maskd", [128, 64], F32, kind="ExternalInput").ap()
    maskb = nc.dram_tensor("maskb", [128, 1], F32, kind="ExternalInput").ap()
    bvalid = nc.dram_tensor("bvalid", [1, BW], F32, kind="ExternalInput").ap()
    identd = nc.dram_tensor("identd", [128, 128], F32, kind="ExternalInput").ap()
    out = nc.dram_tensor("out", [C, 2048], F32, kind="ExternalOutput").ap()
    h1hbm = nc.dram_tensor("h1hbm", [C, 2048], F32, kind="Internal").ap()
    hbscr = nc.dram_tensor("hbscr", [128, NCH * BW], F32, kind="Internal").ap()

    sch = Sch()

    with contextlib.ExitStack() as top:
        E = top.enter_context
        sems = {s: E(nc.semaphore(s))
                for s in ("gd", "sd", "pe", "ac", "dv", "ge", "go", "gx",
                          "se", "so", "hs0", "hs1")}

        guard0 = E(nc.sbuf_tensor("guard0", [128, 64], F32))
        h1bnd = E(nc.sbuf_tensor("h1bnd", [128, NCH * BW], F32))
        guard1 = E(nc.sbuf_tensor("guard1", [128, 64], F32))
        kv_scope = E(contextlib.ExitStack())
        kfm = kv_scope.enter_context(
            nc.sbuf_tensor("kfm", [128, NCH * T], BF16, side="right"))
        vtm = kv_scope.enter_context(
            nc.sbuf_tensor("vtm", [128, 32 * C], BF16, side="right"))
        bq = E(nc.sbuf_tensor("bq", [128, 8], F32))
        bk = E(nc.sbuf_tensor("bk", [128, 8], F32))
        bv = E(nc.sbuf_tensor("bv", [128, 8], F32))
        bp = E(nc.sbuf_tensor("bp", [128, 8], F32))
        beff = E(nc.sbuf_tensor("beff", [128, 8], F32))
        mur = E(nc.sbuf_tensor("mur", [128, 8], F32))
        muk = E(nc.sbuf_tensor("muk", [128, 8], F32))
        mskd = E(nc.sbuf_tensor("mskd", [128, 64], BF16))
        mskb = E(nc.sbuf_tensor("mskb", [128, 1], BF16))
        bvrow = E(nc.sbuf_tensor("bvrow", [1, BW], F32))
        bvbc = E(nc.sbuf_tensor("bvbc", [128, BW], F32))
        identf = E(nc.sbuf_tensor("identf", [128, 128], F32))
        onesb = E(nc.sbuf_tensor("onesb", [128, 2], BF16))
        bvb = E(nc.sbuf_tensor("bvb", [128, 8], BF16))
        onesf = E(nc.sbuf_tensor("onesf", [128, 128], F32))
        negB = E(nc.sbuf_tensor("negB", [128, 8], F32))
        scal = E(nc.sbuf_tensor("scal", [128, 16], F32))
        rowsS = E(nc.sbuf_tensor("rowsS", [128, 1536], F32))
        rowsR = E(nc.sbuf_tensor("rowsR", [128, 512], F32))
        

        gmax = E(nc.sbuf_tensor("gmax", [128, 2], F32))

        pst = [E(nc.psum_tensor(f"ps{i}", [128, 1024], F32)) for i in range(4)]

        def bank(i):
            return pst[i // 2][:, 512 * (i % 2):512 * (i % 2) + 512]

        bready = None
        bfree = [("dv", 0)] * 8

        def gdma(dst, src, waits=(), sem="gd"):
            return sch.op("gpsimd", lambda g, d=dst, s=src: g.dma_start(out=d, in_=s),
                          waits, inc=sem, amt=16)

        def sdma(dst, src, waits=(), sem="sd"):
            return sch.op("sync", lambda s_, d=dst, s=src: s_.dma_start(out=d, in_=s),
                          waits, inc=sem, amt=16)

        def mm(o, l, r, st_, sp, tp=None, waits=(), inc=False):
            if tp is not None and l.dtype != F32:
                def fn(t, o=o, l=l, r=r, st_=st_, sp=sp, tp=tp):
                    t.ldweights(l, tile_position=tp)
                    return t.matmul(o, l, r, start=st_, stop=sp,
                                    tile_position=tp)
            else:
                def fn(t, o=o, l=l, r=r, st_=st_, sp=sp, tp=tp):
                    return t.matmul(o, l, r, start=st_, stop=sp,
                                    tile_position=tp)
            return sch.op("tensor", fn, waits, inc="pe" if inc else None)

        def tpose(o, i_, waits=()):
            return sch.op("tensor",
                          lambda t, o=o, i_=i_: t.transpose(o, i_, identf[:]),
                          waits, inc="pe")

        def act(fn, waits=(), inc=True):
            return sch.op("scalar", fn, waits, inc="ac" if inc else None)

        def dve(fn, waits=(), inc=True):
            return sch.op("vector", fn, waits, inc="dv" if inc else None)

        with contextlib.ExitStack() as p1s:
            E1 = p1s.enter_context
            xb = E1(nc.sbuf_tensor("xb", [128, NCH * 2048], BF16))
            wkb = E1(nc.sbuf_tensor("wkb", [128, NCH * 1024], BF16))
            wvb = E1(nc.sbuf_tensor("wvb", [128, NCH * 1024], BF16))
            wpb1 = E1(nc.sbuf_tensor("wpb1", [128, 2 * 1024], BF16))

            for dst, src in ((bq[:, :], bqc[:, :]), (bk[:, :], bkc[:, :]),
                             (bv[:, :], bvc[:, :]), (bp[:, :], bpc[:, :]),
                             (mur[:, :], murc[:, :]), (muk[:, :], mukc[:, :]),
                             (mskd[:, :], maskd[:, :]), (mskb[:, :], maskb[:, :]),
                             (bvrow[:, :], bvalid[:, :]), (identf[:, :], identd[:, :])):
                gdma(dst, src)
            tok_consts = sch.n["gd"]
            dve(lambda v: v.memset(onesb[:], 1.0), inc=False)
            tok_ones = dve(lambda v: v.memset(onesf[:], 1.0))

            for c in range(NCH):
                gdma(wkb[:, 1024 * c:1024 * (c + 1)],
                     wA[128 * c:128 * (c + 1), 1024:2048])
                gdma(wvb[:, 1024 * c:1024 * (c + 1)],
                     wA[128 * c:128 * (c + 1), 2048:3072])
            tok_wkv = sch.n["gd"]

            t_ = mm(bank(6)[:, 0:BW], onesf[0:1, 0:128], bvrow[0:1, :], True, True,
                    waits=[("gd", tok_consts), ("dv", tok_ones)], inc=True)
            ct = dve(lambda v: v.tensor_copy(bvbc[:], bank(6)[:, 0:BW]),
                     waits=[("pe", t_)])
            bfree[6] = ("dv", ct)

            pe_end_prev_half = 0
            for h in range(2):
                xtoks = []
                for c in range(NCH):
                    xtoks.append(gdma(
                        xb[:, 2048 * c:2048 * (c + 1)],
                        xT[128 * c:128 * (c + 1), 2048 * h:2048 * (h + 1)],
                        waits=[("pe", pe_end_prev_half)] if c == 0 else ()))
                for tb in range(4):
                    for grp in range(2):
                        for c in range(NCH):
                            for j in range(4):
                                jt = 4 * grp + j
                                w = []
                                if c == 0 and j == 0:
                                    w = [("gd", max(tok_wkv, xtoks[-1]))] + \
                                        [bfree[4 * grp + jj] for jj in range(4)]
                                mm(bank(4 * grp + j),
                                   wkb[:, 1024 * c + 128 * jt:1024 * c + 128 * jt + 128],
                                   xb[:, 2048 * c + 512 * tb:2048 * c + 512 * tb + 512],
                                   c == 0, c == NCH - 1, waits=w,
                                   inc=(c == NCH - 1 and j == 3))
                        ptok = sch.n["pe"]
                        for j in range(4):
                            jt = 4 * grp + j
                            at = act(lambda s_, jt=jt, bi=4 * grp + j, h=h, tb=tb:
                                     s_.activation(
                                         kfm[:, T * jt + 2048 * h + 512 * tb:
                                             T * jt + 2048 * h + 512 * tb + 512],
                                         bank(bi), AF.Identity,
                                         bias=bk[:, jt:jt + 1], scale=1.0),
                                     waits=[("pe", ptok)])
                            bfree[4 * grp + j] = ("ac", at)
                for tt in range(16):
                    b0 = (tt % 2) * 2
                    for c in range(NCH):
                        for jb in range(2):
                            w = []
                            if c == 0 and jb == 0:
                                w = [bfree[b0], bfree[b0 + 1]]
                            mm(bank(b0 + jb),
                               xb[:, 2048 * c + 128 * tt:2048 * c + 128 * tt + 128],
                               wvb[:, 1024 * c + 512 * jb:1024 * c + 512 * jb + 512],
                               c == 0, c == NCH - 1, waits=w,
                               inc=(c == NCH - 1 and jb == 1))
                    ptok = sch.n["pe"]
                    at = act(lambda s_, tt=tt, h=h, pi=b0 // 2:
                             s_.activation(vtm[:, 1024 * (16 * h + tt):
                                               1024 * (16 * h + tt + 1)],
                                           pst[pi][:], AF.Identity,
                                           bias=0.0, scale=1.0),
                             waits=[("pe", ptok)])
                    bfree[b0] = ("ac", at)
                    bfree[b0 + 1] = ("ac", at)
                pe_end_prev_half = sch.n["pe"]

            bvb_tok = dve(lambda v: v.tensor_copy(bvb[:, :], bv[:, :]),
                          waits=[("gd", tok_consts)])
            fc_tok = [0] * NCH
            for fc in range(NCH):
                w = [("pe", fc_tok[fc - 2])] if fc >= 2 else ()
                ld = gdma(wpb1[:, 1024 * (fc % 2):1024 * (fc % 2) + 1024],
                          wP[128 * fc:128 * (fc + 1), :], waits=w,
                          sem="ge" if fc % 2 == 0 else "go")
                for jt in range(NCH):
                    w = []
                    if jt == 0:
                        w = [("ge" if fc % 2 == 0 else "go", ld)]
                        if fc == 0:
                            w += [bfree[b] for b in range(8)] + [("dv", bvb_tok)]
                    fc_tok[fc] = mm(
                        bank(jt)[:, 0:1],
                        wpb1[:, 1024 * (fc % 2) + 128 * jt:
                             1024 * (fc % 2) + 128 * jt + 128],
                        bvb[:, fc:fc + 1],
                        fc == 0, fc == NCH - 1, waits=w,
                        inc=(jt == NCH - 1))
            dt_ = 0
            for jt in range(NCH):
                dt_ = dve(lambda v, jt=jt: v.tensor_add(
                    beff[:, jt:jt + 1], bank(jt)[:, 0:1], bp[:, jt:jt + 1]),
                    waits=[("pe", fc_tok[NCH - 1])] if jt == 0 else ())
                bfree[jt] = ("dv", dt_)

            sch.barrier()
            sch.flush(nc, sems)
        _run_p1 = go("p1")

        h1_store_tok = [(0, 0)] * NQB
        with contextlib.ExitStack() as p2s:
            E2 = p2s.enter_context
            wqb = E2(nc.sbuf_tensor("wqb", [128, 2 * 1024], BF16))
            wpb = wqb
            qfm = E2(nc.sbuf_tensor("qfm", [128, NCH * QBW], BF16))
            xqb = E2(nc.sbuf_tensor("xqb", [128, NCH * QBW], BF16))
            pt0 = E2(nc.sbuf_tensor("pt0", [128, 1024], BF16))
            pt1 = E2(nc.sbuf_tensor("pt1", [128, 1024], BF16))
            y_all = E2(nc.sbuf_tensor("y_all", [128, NCH * QBW], BF16))
            bc_sb = E2(nc.sbuf_tensor("bc_sb", [128, 512], F32))
            xresc = E2(nc.sbuf_tensor("xresc", [128, 1024], F32))
            zc = E2(nc.sbuf_tensor("zc", [128, 1024], F32))
            t1 = E2(nc.sbuf_tensor("t1", [128, NCH * QBW], F32))
            h1pp = E2(nc.sbuf_tensor("h1pp", [128, 1024], F32))
            qbnd = E2(nc.sbuf_tensor("qbnd", [128, NCH * BW], BF16))
            xbndb = E2(nc.sbuf_tensor("xbndb", [128, NCH * BW], BF16))
            xbndf = E2(nc.sbuf_tensor("xbndf", [128, NCH * BW], F32))
            ybnd = E2(nc.sbuf_tensor("ybnd", [128, NCH * BW], BF16))
            tb1 = E2(nc.sbuf_tensor("tb1", [128, NCH * BW], F32))
            pt = (pt0, pt1)
            normsb = t1

            def emit_maxnorm2(rounds, W, dstcol):
                gz = dve(lambda v: v.memset(gmax[:, 0:1], 0.0))
                cap = 4 * (512 // W)
                per_wave = min(8, 2 * cap)
                i0 = 0
                first_wave = True
                while i0 < len(rounds):
                    wave = rounds[i0:i0 + per_wave]
                    n6 = min(len(wave), cap)
                    n7 = len(wave) - n6
                    for i, (rhs, rw) in enumerate(wave):
                        bi = 6 + (0 if i < cap else 1)
                        slot = i % cap
                        row = 32 * (slot % 4)
                        col = W * (slot // 4)
                        w = list(rw)
                        if first_wave and i == 0:
                            w += [("dv", gz)]
                        if i == 0:
                            w += [bfree[6], bfree[7]]
                        mm(bank(bi)[row:row + 1, col:col + W],
                           rhs[0], rhs[1], True, True, tp=(rhs[2], row), waits=w,
                           inc=(i == len(wave) - 1))
                    first_wave = False
                    ptok = sch.n["pe"]
                    w6 = W * ((n6 + 3) // 4)
                    w7 = W * ((n7 + 3) // 4)
                    ct = dve(lambda v, w6=w6: v.tensor_copy(
                        normsb[:, 0:w6], bank(6)[:, 0:w6]), waits=[("pe", ptok)])
                    if n7 > 0:
                        ct = dve(lambda v, w6=w6, w7=w7: v.tensor_copy(
                            normsb[:, w6:w6 + w7], bank(7)[:, 0:w7]))
                    bfree[6] = ("dv", ct)
                    bfree[7] = ("dv", ct)
                    total_w = w6 + w7
                    mt = None
                    start = 0
                    while start < total_w:
                        tw = min(128, total_w - start)
                        w = [("dv", ct)]
                        if mt is None:
                            w += [bfree[6]]
                        else:
                            w = [("dv", mt)]
                        tt_ = tpose(bank(6)[0:tw, 0:128],
                                    normsb[:, start:start + tw], waits=w)
                        dve(lambda v, tw=tw: v.tensor_reduce(
                            gmax[0:tw, 1:2], bank(6)[0:tw, 0:128:32], AX.X,
                            ALU.max, False, False, False),
                            waits=[("pe", tt_)], inc=False)
                        mt = dve(lambda v, tw=tw: v.tensor_max(
                            gmax[0:tw, 0:1], gmax[0:tw, 0:1], gmax[0:tw, 1:2]))
                        bfree[6] = ("dv", mt)
                        start += tw
                    i0 += per_wave
                w = [("dv", sch.n["dv"]), bfree[6]]
                tt_ = tpose(bank(6)[0:1, 0:128], gmax[:, 0:1], waits=w)
                rt = dve(lambda v, dstcol=dstcol: v.tensor_reduce(
                    scal[0:1, dstcol:dstcol + 1], bank(6)[0:1, 0:128], AX.X,
                    ALU.max, False, False, False), waits=[("pe", tt_)])
                bfree[6] = ("dv", rt)
                return rt

            def emit_negB(qcol, dstcol):
                dt_ = dve(lambda v, qcol=qcol: v.tensor_mul(
                    scal[0:1, 1:2], scal[0:1, qcol:qcol + 1], scal[0:1, 0:1]),
                    inc=False)
                dt_ = dve(lambda v: v.tensor_scalar(
                    scal[0:1, 1:2], scal[0:1, 1:2], 1e-30, None, ALU.add))
                lt = act(lambda s_: s_.activation(scal[0:1, 2:3], scal[0:1, 1:2],
                                                  AF.Ln, bias=0.0, scale=1.0),
                         waits=[("dv", dt_)], inc=False)
                lt = act(lambda s_: s_.activation(scal[0:1, 3:4], scal[0:1, 2:3],
                                                  AF.Exp, bias=0.0, scale=0.5))
                dt_ = dve(lambda v: v.tensor_scalar(
                    scal[0:1, 4:5], scal[0:1, 3:4], -0.125, None, ALU.mult),
                    waits=[("ac", lt)])
                t_ = mm(bank(7)[:, 0:1], onesf[0:1, 0:128], scal[0:1, 4:5],
                        True, True, waits=[("dv", dt_), bfree[7]], inc=True)
                ct = dve(lambda v, dstcol=dstcol: v.tensor_copy(
                    negB[:, dstcol:dstcol + 1], bank(7)[:, 0:1]),
                    waits=[("pe", t_)])
                bfree[7] = ("dv", ct)
                return ct

            def emit_q(W, xsrc_hbm, qdst, xb_dst):
                xt = gdma(xb_dst[:].rearrange("p (c t) -> p c t", c=NCH), xsrc_hbm,
                          sem="gx")
                ctoks = [0] * NCH
                wtoks = []
                for c in range(NCH):
                    w = [("pe", ctoks[c - 2])] if c >= 2 else ()
                    wtoks.append(gdma(wqb[:, 1024 * (c % 2):1024 * (c % 2) + 1024],
                                      wA[128 * c:128 * (c + 1), 0:1024], waits=w,
                                      sem="ge" if c % 2 == 0 else "go"))
                    for jt in range(NCH):
                        w = []
                        if jt == 0:
                            w = [("ge" if c % 2 == 0 else "go", wtoks[c])]
                            if c == 0:
                                w += [("gx", xt)] + [bfree[b] for b in range(8)]
                        ctoks[c] = mm(
                            bank(jt)[:, 0:W],
                            wqb[:, 1024 * (c % 2) + 128 * jt:
                                1024 * (c % 2) + 128 * jt + 128],
                            xb_dst[:, W * c:W * c + W],
                            c == 0, c == NCH - 1, waits=w, inc=(jt == NCH - 1))
                ptok = ctoks[NCH - 1]
                for jt in range(NCH):
                    at = act(lambda s_, jt=jt, W=W, qdst=qdst: s_.activation(
                        qdst[:, W * jt:W * jt + W], bank(jt)[:, 0:W], AF.Identity,
                        bias=bq[:, jt:jt + 1], scale=1.0), waits=[("pe", ptok)])
                    bfree[jt] = ("ac", at)
                return ("ac", sch.n["ac"])

            def emit_attention(W, m, qsrc, ydst, negcol, bnd):
                for hp in range(NCH):
                    if ATT_BARRIER:
                        sch.barrier()
                    pt_cons = [("pe", 0), ("pe", 0)]
                    first_av = True
                    ktmax = NKT if bnd else 8 * m + 8
                    for kt in range(ktmax):
                        if ATT_BARRIER:
                            sch.barrier()
                        par = kt % 2
                        f0 = 0 if bnd else max(0, kt - 8 * m) * 64
                        sb = 2 * par
                        w = [bfree[sb], bfree[sb + 1]]
                        mm(bank(sb)[:, f0:W],
                           kfm[0:64, T * hp + 128 * kt:T * hp + 128 * kt + 128],
                           qsrc[0:64, W * hp + f0:W * hp + W],
                           True, True, tp=(0, 0), waits=w)
                        st_ = mm(bank(sb + 1)[:, f0:W],
                                 kfm[64:128, T * hp + 128 * kt:T * hp + 128 * kt + 128],
                                 qsrc[64:128, W * hp + f0:W * hp + W],
                                 True, True, tp=(64, 0), inc=True)
                        et = act(lambda s_, par=par, f0=f0, W=W, negcol=negcol:
                                 s_.activation(
                                     pt[par][:].rearrange("p (a n) -> p a n", a=2)
                                     [:, :, f0:W],
                                     pst[par][:].rearrange("p (a n) -> p a n", a=2)
                                     [:, :, f0:W],
                                     AF.Exp, bias=negB[:, negcol:negcol + 1],
                                     scale=0.125),
                                 waits=[("pe", st_), pt_cons[par]])
                        bfree[sb] = ("ac", et)
                        bfree[sb + 1] = ("ac", et)
                        mt = None
                        if bnd:
                            if kt > 0:
                                dve(lambda v, par=par, kt=kt: v.memset(
                                    pt[par][:, 0:kt], 0.0),
                                    waits=[("ac", et)], inc=False)
                                dve(lambda v, par=par, kt=kt: v.memset(
                                    pt[par][:, 512:512 + kt], 0.0),
                                    inc=False)
                            mt = dve(lambda v, par=par, kt=kt: v.tensor_mul(
                                pt[par][:, kt:kt + 1], pt[par][:, kt:kt + 1],
                                mskb[:, 0:1]), waits=[("ac", et)], inc=False)
                            mt = dve(lambda v, par=par, kt=kt: v.tensor_mul(
                                pt[par][:, 512 + kt:512 + kt + 1],
                                pt[par][:, 512 + kt:512 + kt + 1], mskb[:, 0:1]))
                        elif kt >= 8 * m:
                            mt = dve(lambda v, par=par, f0=f0: v.tensor_mul(
                                pt[par][:, f0:f0 + 64], pt[par][:, f0:f0 + 64],
                                mskd[:, :]), waits=[("ac", et)], inc=False)
                            mt = dve(lambda v, par=par, f0=f0: v.tensor_mul(
                                pt[par][:, 512 + f0:512 + f0 + 64],
                                pt[par][:, 512 + f0:512 + f0 + 64], mskd[:, :]))
                        w = [("ac", et)] if mt is None else [("dv", mt)]
                        if first_av:
                            w = w + [bfree[4], bfree[5], bfree[6], bfree[7]]
                        mm(bank(4)[0:64, f0:W],
                           vtm[:, 1024 * kt + 128 * hp:1024 * kt + 128 * hp + 64],
                           pt[par][:, f0:W], first_av, kt == ktmax - 1,
                           tp=(0, 0), waits=w)
                        mm(bank(5)[64:128, f0:W],
                           vtm[:, 1024 * kt + 128 * hp + 64:1024 * kt + 128 * hp + 128],
                           pt[par][:, 512 + f0:512 + W], first_av, kt == ktmax - 1,
                           tp=(0, 64))
                        mm(bank(6)[0:1, f0:W], onesb[:, 0:1], pt[par][:, f0:W],
                           first_av, kt == ktmax - 1, tp=(0, 0))
                        ptok = mm(bank(7)[0:1, f0:W], onesb[:, 1:2],
                                  pt[par][:, 512 + f0:512 + W],
                                  first_av, kt == ktmax - 1, tp=(0, 0), inc=True)
                        first_av = False
                        pt_cons[par] = ("pe", ptok)
                    if DEBUG_DUMP == "ysb" and bnd:
                        yct = dve(lambda v, W=W: v.tensor_copy(
                            zc[0:64, 0:W], bank(4)[0:64, 0:W]),
                            waits=[("pe", ptok)], inc=False)
                        yct = dve(lambda v, W=W: v.tensor_copy(
                            zc[64:128, 0:W], bank(5)[64:128, 0:W]), inc=False)
                        yct = dve(lambda v, W=W: v.tensor_copy(
                            xresc[0:1, 0:W], bank(6)[0:1, 0:W]), inc=False)
                        yct = dve(lambda v, W=W: v.tensor_copy(
                            xresc[32:33, 0:W], bank(7)[0:1, 0:W]))
                        sdma(out[128 * hp:128 * hp + 128, 0:W], zc[:, 0:W],
                             waits=[("dv", yct)])
                        sdma(out[128 * hp:128 * hp + 1, 64:64 + W],
                             xresc[0:1, 0:W], waits=[("dv", yct)])
                        sdma(out[128 * hp + 64:128 * hp + 65, 64:64 + W],
                             xresc[32:33, 0:W], waits=[("dv", yct)])
                        if hp == 0:
                            sdma(out[1016:1017, 2040:2048], negB[0:1, 0:8],
                                 waits=[("dv", yct)])
                    if DEBUG_DUMP == "ysum0" and not bnd and m == 0:
                        yct = dve(lambda v, W=W: v.tensor_copy(
                            zc[0:64, 0:W], bank(4)[0:64, 0:W]),
                            waits=[("pe", ptok)], inc=False)
                        yct = dve(lambda v, W=W: v.tensor_copy(
                            zc[64:128, 0:W], bank(5)[64:128, 0:W]), inc=False)
                        yct = dve(lambda v, W=W: v.tensor_copy(
                            xresc[0:1, 0:W], bank(6)[0:1, 0:W]), inc=False)
                        yct = dve(lambda v, W=W: v.tensor_copy(
                            xresc[32:33, 0:W], bank(7)[0:1, 0:W]))
                        sdma(out[128 * hp:128 * hp + 128, 0:W], zc[:, 0:W],
                             waits=[("dv", yct)])
                        sdma(out[128 * hp:128 * hp + 1, QBW:QBW + W],
                             xresc[0:1, 0:W], waits=[("dv", yct)])
                        sdma(out[128 * hp + 64:128 * hp + 65, QBW:QBW + W],
                             xresc[32:33, 0:W], waits=[("dv", yct)])
                        if hp == 0:
                            sdma(out[1016:1017, 2040:2048], negB[0:1, 0:8],
                                 waits=[("dv", yct)])
                    if bnd:
                        ptok = mm(bank(7)[0:1, 64:65], onesb[:, 0:1],
                                  onesb[:, 1:2], True, True, tp=(0, 0),
                                  inc=True)
                        sch.barrier()
                    r1 = dve(lambda v, W=W: v.tensor_scalar(
                        t1[0:1, 0:W], bank(6)[0:1, 0:W], 1e-30, None, ALU.add),
                        waits=[("pe", ptok)], inc=False)
                    r1 = dve(lambda v, W=W: v.tensor_scalar(
                        t1[0:1, 512:512 + W], bank(7)[0:1, 0:W], 1e-30,
                        None, ALU.add))
                    bfree[6] = ("dv", r1)
                    bfree[7] = ("dv", r1)
                    a1 = act(lambda s_, W=W: s_.activation(
                        t1[0:1, 1024:1024 + W], t1[0:1, 0:W], AF.Ln,
                        bias=0.0, scale=1.0), waits=[("dv", r1)], inc=False)
                    a1 = act(lambda s_, W=W: s_.activation(
                        rowsR[0:1, 0:W], t1[0:1, 1024:1024 + W], AF.Exp,
                        bias=0.0, scale=-1.0), inc=False)
                    a1 = act(lambda s_, W=W: s_.activation(
                        t1[0:1, 1536:1536 + W], t1[0:1, 512:512 + W], AF.Ln,
                        bias=0.0, scale=1.0), inc=False)
                    a1 = act(lambda s_, W=W: s_.activation(
                        rowsS[0:1, 1024:1024 + W], t1[0:1, 1536:1536 + W],
                        AF.Exp, bias=0.0, scale=-1.0))
                    if bnd:
                        mm(bank(0)[0:64, 0:W], onesf[0:1, 0:64],
                           rowsR[0:1, 0:W], True, True, tp=(0, 0),
                           waits=[("ac", a1), bfree[0]])
                        mm(bank(0)[64:128, 0:W], onesf[0:1, 0:64],
                           rowsS[0:1, 1024:1024 + W], True, True, tp=(0, 64))
                        w_bc = ()
                    else:
                        w_bc = [("ac", a1), bfree[0]]
                    mm(bank(0)[0:64, 0:W], onesf[0:1, 0:64], rowsR[0:1, 0:W],
                       True, True, tp=(0, 0), waits=w_bc)
                    bt = mm(bank(0)[64:128, 0:W], onesf[0:1, 0:64],
                            rowsS[0:1, 1024:1024 + W], True, True, tp=(0, 64),
                            inc=True)
                    if bnd:
                        sch.barrier()
                    if bnd:
                        dve(lambda v, W=W: v.tensor_copy(bc_sb[:, 0:W],
                                                         bank(0)[:, 0:W]),
                            waits=[("pe", bt)], inc=False)
                    ct = dve(lambda v, W=W: v.tensor_copy(bc_sb[:, 0:W],
                                                          bank(0)[:, 0:W]),
                             waits=[("pe", bt)])
                    bfree[0] = ("dv", ct)
                    if DEBUG_DUMP == "ysb" and bnd and hp == int(DEBUG_HP):
                        pct = dve(lambda v, W=W: v.tensor_copy(
                            zc[0:1, 0:W], bank(6)[0:1, 0:W]), inc=False)
                        pct = dve(lambda v, W=W: v.tensor_copy(
                            zc[0:1, W:2 * W], bank(7)[0:1, 0:W]))
                        dtok = sdma(out[0:128, 1500:1500 + W], bc_sb[:, 0:W],
                                    waits=[("dv", ct)])
                        dtok = sdma(out[128:256, 1600:1600 + W], rowsR[:, 0:W],
                                    waits=[("dv", ct)])
                        dtok = sdma(out[256:384, 1700:1700 + W],
                                    rowsS[:, 1024:1024 + W], waits=[("dv", ct)])
                        dtok = sdma(out[384:512, 1800:1800 + 2 * W], zc[:, 0:2 * W],
                                    waits=[("dv", pct)])
                        dve(lambda v: v.engine_nop(), waits=[("sd", dtok)])
                    yt = dve(lambda v, hp=hp, W=W, ydst=ydst: v.tensor_mul(
                        ydst[0:64, W * hp:W * hp + W], bank(4)[0:64, 0:W],
                        bc_sb[0:64, 0:W]), inc=False)
                    yt = dve(lambda v, hp=hp, W=W, ydst=ydst: v.tensor_mul(
                        ydst[64:128, W * hp:W * hp + W], bank(5)[64:128, 0:W],
                        bc_sb[64:128, 0:W]))
                    bfree[4] = ("dv", yt)
                    bfree[5] = ("dv", yt)

            def emit_tail(W, ysrc, xres_hbm, to_hbm, m):
                ctoks = [0] * NCH
                wtoks = []
                for fc in range(NCH):
                    w = [("pe", ctoks[fc - 2])] if fc >= 2 else ()
                    wtoks.append(gdma(wpb[:, 1024 * (fc % 2):1024 * (fc % 2) + 1024],
                                      wP[128 * fc:128 * (fc + 1), :], waits=w,
                                      sem="ge" if fc % 2 == 0 else "go"))
                    for jt in range(NCH):
                        w = []
                        if jt == 0:
                            w = [("ge" if fc % 2 == 0 else "go", wtoks[fc])]
                            if fc == 0:
                                w += [bfree[b] for b in range(8)]
                        ctoks[fc] = mm(
                            bank(jt)[:, 0:W],
                            wpb[:, 1024 * (fc % 2) + 128 * jt:
                                1024 * (fc % 2) + 128 * jt + 128],
                            ysrc[:, W * fc:W * fc + W],
                            fc == 0, fc == NCH - 1, waits=w, inc=(jt == NCH - 1))
                ptok = ctoks[NCH - 1]
                if W == BW:
                    sch.barrier()
                tsrc = t1 if to_hbm else tb1
                add_tok = [0, 0]
                xr_tok = [0, 0]
                evac_tok = [0] * NCH
                for jt in range(NCH):
                    par = jt % 2
                    if to_hbm:
                        xr_tok[par] = sdma(
                            xresc[:, 512 * par:512 * par + W],
                            xres_hbm[128 * jt:128 * (jt + 1), :],
                            waits=[("dv", add_tok[par])] if jt >= 2 else (),
                            sem="se" if par == 0 else "so")
                    evac_tok[jt] = act(lambda s_, jt=jt, par=par, W=W: s_.activation(
                        zc[:, 512 * par:512 * par + W], bank(jt)[:, 0:W],
                        AF.Identity, bias=beff[:, jt:jt + 1], scale=1.0),
                        waits=[("pe", ptok), ("dv", add_tok[par])])
                    bfree[jt] = ("ac", evac_tok[jt])
                    if to_hbm:
                        add_tok[par] = dve(
                            lambda v, jt=jt, par=par, W=W: v.tensor_add(
                                t1[:, W * jt:W * jt + W],
                                zc[:, 512 * par:512 * par + W],
                                xresc[:, 512 * par:512 * par + W]),
                            waits=[("ac", evac_tok[jt]),
                                   ("se" if par == 0 else "so", xr_tok[par])])
                    else:
                        add_tok[par] = dve(
                            lambda v, jt=jt, par=par, W=W: v.tensor_add(
                                tb1[:, W * jt:W * jt + W],
                                zc[:, 512 * par:512 * par + W],
                                xbndf[:, W * jt:W * jt + W]),
                            waits=[("ac", evac_tok[jt])])
                if W == BW:
                    sch.barrier()
                cast_guard = [0, 0]
                sq_guard = [0, 0]
                stok = 0
                for jt in range(NCH):
                    par = jt % 2
                    ct = dve(lambda v, jt=jt, par=par, W=W, tsrc=tsrc: v.tensor_copy(
                        pt0[:, 512 * par:512 * par + W], tsrc[:, W * jt:W * jt + W]),
                        waits=[("pe", cast_guard[par]), ("ac", sq_guard[par])] +
                        ([("dv", add_tok[jt % 2])] if jt == 0 else []))
                    sq = act(lambda s_, par=par, W=W: s_.activation(
                        pt1[:, 512 * par:512 * par + W],
                        pt0[:, 512 * par:512 * par + W],
                        AF.Square, bias=0.0, scale=1.0),
                        waits=[("dv", ct), ("pe", cast_guard[par])])
                    sq_guard[par] = sq
                    w = [("dv", ct)]
                    if jt == 0:
                        w += [bfree[0], ("ac", evac_tok[0])]
                    mm(bank(0)[0:1, 0:W], onesb[:, 0:1],
                       pt0[:, 512 * par:512 * par + W],
                       jt == 0, jt == NCH - 1, tp=(0, 0), waits=w)
                    w = [("ac", sq)]
                    if jt == 0:
                        w += [bfree[1], ("ac", evac_tok[1])]
                    stok = mm(bank(1)[0:1, 0:W], onesb[:, 0:1],
                              pt1[:, 512 * par:512 * par + W],
                              jt == 0, jt == NCH - 1, tp=(0, 0), waits=w, inc=True)
                    cast_guard[par] = stok
                if W == BW:
                    stok = mm(bank(1)[0:1, 64:65], onesb[:, 0:1], onesb[:, 1:2],
                              True, True, tp=(0, 0), inc=True)
                    sch.barrier()
                rt = dve(lambda v, W=W: v.tensor_scalar(
                    rowsS[0:1, 0:W], bank(0)[0:1, 0:W], 1.0 / C, None, ALU.mult),
                    waits=[("pe", stok)], inc=False)
                rt = dve(lambda v, W=W: v.tensor_scalar(
                    rowsS[0:1, 512:512 + W], bank(1)[0:1, 0:W], 1.0 / C, None,
                    ALU.mult), inc=False)
                rt = dve(lambda v, W=W: v.tensor_mul(
                    rowsS[0:1, 1024:1024 + W], rowsS[0:1, 0:W], rowsS[0:1, 0:W]),
                    inc=False)
                rt = dve(lambda v, W=W: v.scalar_tensor_tensor(
                    bc_sb[0:1, 0:W], rowsS[0:1, 512:512 + W], EPS,
                    rowsS[0:1, 1024:1024 + W], ALU.add, ALU.subtract))
                bfree[0] = ("dv", rt)
                bfree[1] = ("dv", rt)
                lt = act(lambda s_, W=W: s_.activation(
                    rowsS[0:1, 512:512 + W], bc_sb[0:1, 0:W], AF.Ln,
                    bias=0.0, scale=1.0), waits=[("dv", rt)], inc=False)
                lt = act(lambda s_, W=W: s_.activation(
                    rowsS[0:1, 1024:1024 + W], rowsS[0:1, 512:512 + W], AF.Exp,
                    bias=0.0, scale=-0.5))
                mrt = dve(lambda v, W=W: v.tensor_mul(
                    rowsS[0:1, 512:512 + W], rowsS[0:1, 0:W],
                    rowsS[0:1, 1024:1024 + W]), waits=[("ac", lt)])
                if W == BW:
                    mm(bank(6)[:, 0:W], onesf[0:1, 0:128],
                       rowsS[0:1, 1024:1024 + W], True, True, tp=(0, 0),
                       waits=[("dv", mrt), bfree[6], bfree[7]])
                    mm(bank(7)[:, 0:W], onesf[0:1, 0:128],
                       rowsS[0:1, 512:512 + W], True, True, tp=(0, 0))
                    w_bc = ()
                else:
                    w_bc = [("dv", mrt), bfree[6], bfree[7]]
                mm(bank(6)[:, 0:W], onesf[0:1, 0:128], rowsS[0:1, 1024:1024 + W],
                   True, True, tp=(0, 0), waits=w_bc)
                bt = mm(bank(7)[:, 0:W], onesf[0:1, 0:128],
                        rowsS[0:1, 512:512 + W], True, True, tp=(0, 0), inc=True)
                st_tok = 0
                if W == BW:
                    sch.barrier()
                h1_guard = [0, 0]
                for jt in range(NCH):
                    par = jt % 2
                    dve(lambda v, jt=jt, par=par, W=W, tsrc=tsrc: v.tensor_mul(
                        zc[:, 512 * par:512 * par + W], tsrc[:, W * jt:W * jt + W],
                        bank(6)[:, 0:W]),
                        waits=[("pe", bt),
                               ("hs0" if par == 0 else "hs1", h1_guard[par])],
                        inc=False)
                    ht = dve(lambda v, par=par, W=W: v.tensor_sub(
                        h1pp[:, 512 * par:512 * par + W],
                        zc[:, 512 * par:512 * par + W], bank(7)[:, 0:W]))
                    if to_hbm:
                        st_tok = sdma(h1hbm[128 * jt:128 * (jt + 1),
                                            QBW * m:QBW * m + W],
                                      h1pp[:, 512 * par:512 * par + W],
                                      waits=[("dv", ht)],
                                      sem="hs0" if par == 0 else "hs1")
                        h1_guard[par] = st_tok
                    else:
                        dve(lambda v, jt=jt, par=par, W=W: v.tensor_mul(
                            h1bnd[:, W * jt:W * jt + W],
                            h1pp[:, 512 * par:512 * par + W], bvbc[:, 0:W]))
                bfree[6] = ("dv", sch.n["dv"])
                bfree[7] = ("dv", sch.n["dv"])
                return (sch.n["hs0"], sch.n["hs1"])

            for c in range(NCH) if go("knorm") else []:
                sqt = dve(lambda v, c=c: v.tensor_mul(
                    xqb[:, 0:4096], kfm[:, T * c:T * c + 4096],
                    kfm[:, T * c:T * c + 4096]),
                    waits=[("pe", sch.n["pe"])] if c == 0 else ())
                rounds = [((onesb[64 * a:64 * a + 64, 0:1],
                            xqb[64 * a:64 * a + 64, 512 * blk:512 * blk + 512],
                            64 * a), [("dv", sqt)])
                          for a in range(2) for blk in range(8)]
                emit_maxnorm2(rounds, 512, 1)
                if c == 0:
                    dve(lambda v: v.tensor_copy(scal[0:1, 0:1], scal[0:1, 1:2]))
                else:
                    dve(lambda v: v.tensor_max(scal[0:1, 0:1], scal[0:1, 0:1],
                                               scal[0:1, 1:2]))
            sch.barrier()

            if go("bnd_q"):
                sdma(xbndf[:].rearrange("p (c t) -> p c t", c=NCH),
                     xTb.rearrange("(c p) t -> p c t", p=128))
                emit_q(BW, xTb.rearrange("(c p) t -> p c t", p=128), qbnd, xbndb)
                sch.barrier()
            if go("bnd_norm"):
                sqt = dve(lambda v: v.tensor_mul(xbndb[:], qbnd[:], qbnd[:]))
                rounds = [((onesb[64 * a:64 * a + 64, 0:1],
                            xbndb[64 * a:64 * a + 64, BW * c:BW * c + BW], 64 * a),
                           [("dv", sqt)])
                          for c in range(NCH) for a in range(2)]
                emit_maxnorm2(rounds, BW, 5)
                emit_negB(5, 4)
                sch.barrier()
            if go("bnd_attn"):
                emit_attention(BW, 0, qbnd, ybnd, 4, bnd=True)
                sch.barrier()
            if go("bnd_tail"):
                emit_tail(BW, ybnd, None, False, 0)
                sdma(hbscr[:, :], h1bnd[:, :], waits=[("dv", sch.n["dv"])])
                sch.barrier()

            for m in range(NQB):
                if m == 0:
                    gq, gn, ga, gt = go("qb0_q"), go("qb0_norm"), go("qb0_attn"), go("qb0_tail")
                else:
                    gq = gn = ga = gt = go(f"qb{m}")
                if gq:
                    emit_q(QBW, xTq.rearrange("(c p) t -> p c t", p=128)
                           [:, :, QBW * m:QBW * (m + 1)], qfm, xqb)
                    sch.barrier()
                if gn:
                    sqt = dve(lambda v: v.tensor_mul(xqb[:], qfm[:], qfm[:]))
                    rounds = [((onesb[64 * a:64 * a + 64, 0:1],
                                xqb[64 * a:64 * a + 64, QBW * c:QBW * c + QBW], 64 * a),
                               [("dv", sqt)])
                              for c in range(NCH) for a in range(2)]
                    emit_maxnorm2(rounds, QBW, 6)
                    emit_negB(6, m)
                    sch.barrier()
                if ga:
                    emit_attention(QBW, m, qfm, y_all, m, bnd=False)
                    sch.barrier()
                if gt:
                    h1_store_tok[m] = emit_tail(QBW, y_all,
                                                xTq[:, QBW * m:QBW * (m + 1)], True, m)
                    sch.barrier()

            sch.flush(nc, sems)
        kv_scope.close()

        with contextlib.ExitStack() as p3s:
            E3 = p3s.enter_context
            wrb = E3(nc.sbuf_tensor("wrb", [128, NCH * 1024], BF16))
            wkb3 = E3(nc.sbuf_tensor("wkb3", [128, NCH * 1024], BF16))
            wvb3 = E3(nc.sbuf_tensor("wvb3", [128, NCH * 1024], BF16))
            h1f = E3(nc.sbuf_tensor("h1f", [128, 2 * NCH * QBW], F32))
            ddc = E3(nc.sbuf_tensor("ddc", [128, 512], F32))
            xrb = E3(nc.sbuf_tensor("xrb", [128, NCH * QBW], BF16))
            xkb = E3(nc.sbuf_tensor("xkb", [128, NCH * QBW], BF16))
            r_f = E3(nc.sbuf_tensor("r_f", [128, NCH * QBW], F32))
            ec = E3(nc.sbuf_tensor("ec", [128, 1024], F32))
            rc = E3(nc.sbuf_tensor("rc", [128, 512], F32))
            krelc = E3(nc.sbuf_tensor("krelc", [128, 1024], F32))
            ksq3 = E3(nc.sbuf_tensor("ksq3", [128, NCH * QBW], BF16))
            ffnc = E3(nc.sbuf_tensor("ffnc", [128, 1024], F32))
            t2 = E3(nc.sbuf_tensor("t2", [128, NCH * QBW], F32))
            c1 = E3(nc.sbuf_tensor("c1", [128, 1024], BF16))
            c2 = E3(nc.sbuf_tensor("c2", [128, 1024], BF16))
            h2pp = E3(nc.sbuf_tensor("h2pp", [128, 1024], F32))

            h1bnd3 = E3(nc.sbuf_tensor("h1bnd3", [128, NCH * BW], F32))
            if go("p3_w"):
                sdma(h1bnd3[:, :], hbscr[:, :])
                sch.op("vector", None, waits=[("sd", sch.n["sd"])])
                for mat, dst in ((wRT, wrb), (wKT, wkb3), (wVT, wvb3)):
                    for c in range(NCH):
                        gdma(dst[:, 1024 * c:1024 * (c + 1)],
                             mat[128 * c:128 * (c + 1), :])
            tok_w3 = sch.n["gd"]

            h1_cons = [0, 0]
            for m in range(NQB) if _lim > _stages.index("p3_w") + 0 else []:
                if not go(f"p3_m{m}"):
                    break
                par = m % 2
                hoff = NCH * QBW * par
                htoks = []
                for c in range(NCH):
                    w = [("hs0", h1_store_tok[m][0]), ("hs1", h1_store_tok[m][1])]
                    if c == 0:
                        w += [("dv", h1_cons[par])]
                    htoks.append(sdma(
                        h1f[:, hoff + 512 * c:hoff + 512 * c + 512],
                        h1hbm[128 * c:128 * (c + 1), QBW * m:QBW * (m + 1)],
                        waits=w, sem="se" if c % 2 == 0 else "so"))
                for c in range(NCH):
                    hsl = h1f[:, hoff + 512 * c:hoff + 512 * c + 512]
                    bcols = h1bnd3[:, BW * c + 8 * m:BW * c + 8 * m + 8]
                    dve(lambda v, hsl=hsl: v.tensor_sub(
                        ddc[:, 1:512], hsl[:, 1:512], hsl[:, 0:511]),
                        waits=[("se" if c % 2 == 0 else "so", htoks[c])],
                        inc=False)
                    for s8 in range(8):
                        dve(lambda v, hsl=hsl, bcols=bcols, s8=s8: v.tensor_sub(
                            ddc[:, 64 * s8:64 * s8 + 1],
                            hsl[:, 64 * s8:64 * s8 + 1], bcols[:, s8:s8 + 1]),
                            inc=False)
                    dve(lambda v, c=c, hsl=hsl: v.scalar_tensor_tensor(
                        xrb[:, 512 * c + 1:512 * c + 512], ddc[:, 1:512],
                        mur[:, c:c + 1], hsl[:, 0:511], ALU.mult, ALU.add),
                        inc=False)
                    dve(lambda v, c=c, hsl=hsl: v.scalar_tensor_tensor(
                        xkb[:, 512 * c + 1:512 * c + 512], ddc[:, 1:512],
                        muk[:, c:c + 1], hsl[:, 0:511], ALU.mult, ALU.add),
                        inc=False)
                    for s8 in range(8):
                        dve(lambda v, c=c, bcols=bcols, s8=s8: v.scalar_tensor_tensor(
                            xrb[:, 512 * c + 64 * s8:512 * c + 64 * s8 + 1],
                            ddc[:, 64 * s8:64 * s8 + 1], mur[:, c:c + 1],
                            bcols[:, s8:s8 + 1], ALU.mult, ALU.add), inc=False)
                        dve(lambda v, c=c, bcols=bcols, s8=s8: v.scalar_tensor_tensor(
                            xkb[:, 512 * c + 64 * s8:512 * c + 64 * s8 + 1],
                            ddc[:, 64 * s8:64 * s8 + 1], muk[:, c:c + 1],
                            bcols[:, s8:s8 + 1], ALU.mult, ALU.add), inc=False)
                    dve(lambda v: v.engine_nop())
                mixtok = sch.n["dv"]
                for c in range(NCH):
                    for dt in range(NCH):
                        w = []
                        if c == 0:
                            w = [("dv", mixtok), bfree[dt]] + \
                                ([("gd", tok_w3)] if dt == 0 else [])
                        mm(bank(dt),
                           wrb[:, 1024 * c + 128 * dt:1024 * c + 128 * dt + 128],
                           xrb[:, 512 * c:512 * c + 512],
                           c == 0, c == NCH - 1, waits=w,
                           inc=(c == NCH - 1 and dt == NCH - 1))
                ptok = sch.n["pe"]
                rec_tok = [0, 0]
                for dt in range(NCH):
                    p2_ = dt % 2
                    at = act(lambda s_, dt=dt, p2_=p2_: s_.activation(
                        ec[:, 512 * p2_:512 * p2_ + 512], bank(dt), AF.Exp,
                        bias=0.0, scale=-1.0),
                        waits=[("pe", ptok), ("dv", rec_tok[p2_])])
                    rt = dve(lambda v, p2_=p2_: v.tensor_scalar(
                        rc[:, 0:512], ec[:, 512 * p2_:512 * p2_ + 512], 1.0, None,
                        ALU.add), waits=[("ac", at)])
                    at2 = act(lambda s_, p2_=p2_: s_.activation(
                        ec[:, 512 * p2_:512 * p2_ + 512], rc[:, 0:512], AF.Ln,
                        bias=0.0, scale=1.0), waits=[("dv", rt)], inc=False)
                    at2 = act(lambda s_, dt=dt, p2_=p2_: s_.activation(
                        r_f[:, 512 * dt:512 * dt + 512],
                        ec[:, 512 * p2_:512 * p2_ + 512], AF.Exp,
                        bias=0.0, scale=-1.0))
                    rec_tok[p2_] = rt
                    bfree[dt] = ("ac", at2)
                for c in range(NCH):
                    for dt in range(NCH):
                        w = [bfree[dt]] if c == 0 else []
                        mm(bank(dt),
                           wkb3[:, 1024 * c + 128 * dt:1024 * c + 128 * dt + 128],
                           xkb[:, 512 * c:512 * c + 512],
                           c == 0, c == NCH - 1, waits=w,
                           inc=(c == NCH - 1 and dt == NCH - 1))
                ptok = sch.n["pe"]
                for dt in range(NCH):
                    p2_ = dt % 2
                    act(lambda s_, dt=dt, p2_=p2_: s_.activation(
                        krelc[:, 512 * p2_:512 * p2_ + 512], bank(dt), AF.Relu,
                        bias=0.0, scale=1.0), waits=[("pe", ptok)], inc=False)
                    at = act(lambda s_, dt=dt, p2_=p2_: s_.activation(
                        ksq3[:, 512 * dt:512 * dt + 512],
                        krelc[:, 512 * p2_:512 * p2_ + 512], AF.Square,
                        bias=0.0, scale=1.0))
                    bfree[dt] = ("ac", at)
                sq_all = sch.n["ac"]
                for c in range(NCH):
                    for dt in range(NCH):
                        w = []
                        if c == 0:
                            w = [("ac", sq_all), bfree[dt]]
                        mm(bank(dt),
                           wvb3[:, 1024 * c + 128 * dt:1024 * c + 128 * dt + 128],
                           ksq3[:, 512 * c:512 * c + 512],
                           c == 0, c == NCH - 1, waits=w,
                           inc=(c == NCH - 1 and dt == NCH - 1))
                ptok = sch.n["pe"]
                ft_tok = [0] * NCH
                for dt in range(NCH):
                    p2_ = dt % 2
                    dve(lambda v, dt=dt, p2_=p2_: v.tensor_mul(
                        ffnc[:, 512 * p2_:512 * p2_ + 512], bank(dt),
                        r_f[:, 512 * dt:512 * dt + 512]),
                        waits=[("pe", ptok)], inc=False)
                    ft_tok[dt] = dve(lambda v, dt=dt, p2_=p2_, hoff=hoff: v.tensor_add(
                        t2[:, 512 * dt:512 * dt + 512],
                        ffnc[:, 512 * p2_:512 * p2_ + 512],
                        h1f[:, hoff + 512 * dt:hoff + 512 * dt + 512]))
                    bfree[dt] = ("dv", ft_tok[dt])
                cast_guard = [0, 0]
                sq_guard = [0, 0]
                stok = 0
                for dt in range(NCH):
                    p2_ = dt % 2
                    ct = dve(lambda v, dt=dt, p2_=p2_: v.tensor_copy(
                        c1[:, 512 * p2_:512 * p2_ + 512],
                        t2[:, 512 * dt:512 * dt + 512]),
                        waits=[("pe", cast_guard[p2_]), ("ac", sq_guard[p2_])])
                    sq = act(lambda s_, p2_=p2_: s_.activation(
                        c2[:, 512 * p2_:512 * p2_ + 512],
                        c1[:, 512 * p2_:512 * p2_ + 512], AF.Square,
                        bias=0.0, scale=1.0),
                        waits=[("dv", ct), ("pe", cast_guard[p2_])])
                    sq_guard[p2_] = sq
                    w = [("dv", ct)]
                    if dt == 0:
                        w += [bfree[0], ("dv", ft_tok[0])]
                    mm(bank(0)[0:1, 0:512], onesb[:, 0:1],
                       c1[:, 512 * p2_:512 * p2_ + 512],
                       dt == 0, dt == NCH - 1, tp=(0, 0), waits=w)
                    w = [("ac", sq)]
                    if dt == 0:
                        w += [bfree[1], ("dv", ft_tok[1])]
                    stok = mm(bank(1)[0:1, 0:512], onesb[:, 0:1],
                              c2[:, 512 * p2_:512 * p2_ + 512],
                              dt == 0, dt == NCH - 1, tp=(0, 0), waits=w, inc=True)
                    cast_guard[p2_] = stok
                h1_cons[par] = sch.n["dv"]
                rt = dve(lambda v: v.tensor_scalar(
                    rowsS[0:1, 0:512], bank(0)[0:1, 0:512], 1.0 / C, None,
                    ALU.mult), waits=[("pe", stok)], inc=False)
                rt = dve(lambda v: v.tensor_scalar(
                    rowsS[0:1, 512:1024], bank(1)[0:1, 0:512], 1.0 / C, None,
                    ALU.mult), inc=False)
                rt = dve(lambda v: v.tensor_mul(
                    rowsS[0:1, 1024:1536], rowsS[0:1, 0:512], rowsS[0:1, 0:512]),
                    inc=False)
                rt = dve(lambda v: v.scalar_tensor_tensor(
                    rc[0:1, 0:512], rowsS[0:1, 512:1024], EPS,
                    rowsS[0:1, 1024:1536], ALU.add, ALU.subtract))
                bfree[0] = ("dv", rt)
                bfree[1] = ("dv", rt)
                lt = act(lambda s_: s_.activation(
                    rowsS[0:1, 512:1024], rc[0:1, 0:512], AF.Ln,
                    bias=0.0, scale=1.0), waits=[("dv", rt)], inc=False)
                lt = act(lambda s_: s_.activation(
                    rowsS[0:1, 1024:1536], rowsS[0:1, 512:1024], AF.Exp,
                    bias=0.0, scale=-0.5))
                mrt = dve(lambda v: v.tensor_mul(
                    rowsS[0:1, 512:1024], rowsS[0:1, 0:512],
                    rowsS[0:1, 1024:1536]), waits=[("ac", lt)])
                mm(bank(6)[:, 0:512], onesf[0:1, 0:128], rowsS[0:1, 1024:1536],
                   True, True, tp=(0, 0), waits=[("dv", mrt), bfree[6], bfree[7]])
                bt = mm(bank(7)[:, 0:512], onesf[0:1, 0:128],
                        rowsS[0:1, 512:1024], True, True, tp=(0, 0), inc=True)
                out_guard = [0, 0]
                for dt in range(NCH):
                    p2_ = dt % 2
                    dve(lambda v, dt=dt, p2_=p2_: v.tensor_mul(
                        ffnc[:, 512 * p2_:512 * p2_ + 512],
                        t2[:, 512 * dt:512 * dt + 512], bank(6)[:, 0:512]),
                        waits=[("pe", bt),
                               ("hs0" if p2_ == 0 else "hs1", out_guard[p2_])],
                        inc=False)
                    ht = dve(lambda v, p2_=p2_: v.tensor_sub(
                        h2pp[:, 512 * p2_:512 * p2_ + 512],
                        ffnc[:, 512 * p2_:512 * p2_ + 512], bank(7)[:, 0:512]))
                    out_guard[p2_] = sdma(out[128 * dt:128 * (dt + 1),
                                              QBW * m:QBW * (m + 1)],
                                          h2pp[:, 512 * p2_:512 * p2_ + 512],
                                          waits=[("dv", ht)],
                                          sem="hs0" if p2_ == 0 else "hs1")
                bfree[6] = ("dv", sch.n["dv"])
                bfree[7] = ("dv", sch.n["dv"])

            if DEBUG_DUMP == "hb":
                gdma(out[0:128, 0:NCH * BW], h1bnd[:, :],
                     waits=[("dv", sch.n["dv"]), ("ac", sch.n["ac"])])
                gdma(out[128:256, 0:1536], rowsS[:, :],
                     waits=[("dv", sch.n["dv"]), ("ac", sch.n["ac"])])
                gdma(out[256:384, 0:NCH * BW], tb1[:, :],
                     waits=[("dv", sch.n["dv"]), ("ac", sch.n["ac"])])
                gdma(out[384:512, 0:NCH * BW], ybnd[:, :],
                     waits=[("dv", sch.n["dv"]), ("ac", sch.n["ac"])])
            if DEBUG_DUMP == "q0":
                for hp in range(NCH):
                    gdma(out[128 * hp:128 * (hp + 1), 0:QBW],
                         qfm[:, QBW * hp:QBW * (hp + 1)],
                         waits=[("ac", sch.n["ac"])])
            if DEBUG_DUMP == "k0":
                for hp in range(NCH):
                    gdma(out[128 * hp:128 * (hp + 1), 0:2048],
                         kfm[:, T * hp:T * hp + 2048],
                         waits=[("ac", sch.n["ac"])])
            if DEBUG_DUMP == "v0":
                for tt in range(16):
                    gdma(out[128 * (tt % 8):128 * (tt % 8) + 128,
                             1024 * (tt // 8):1024 * (tt // 8) + 1024],
                         vtm[:, 1024 * tt:1024 * (tt + 1)],
                         waits=[("ac", sch.n["ac"])])
            if DEBUG_DUMP == "y0":
                for hp in range(NCH):
                    gdma(out[128 * hp:128 * (hp + 1), 0:QBW],
                         y_all[:, QBW * hp:QBW * (hp + 1)],
                         waits=[("dv", sch.n["dv"])])
            if DEBUG_DUMP == "h1":
                for c in range(NCH):
                    ht = sdma(h2pp[:, 0:512], h1hbm[128 * c:128 * (c + 1), 0:512],
                              waits=[("sd", max(h1_store_tok))])
                    for m in range(NQB):
                        if m > 0:
                            ht = sdma(h2pp[:, 0:512],
                                      h1hbm[128 * c:128 * (c + 1),
                                            512 * m:512 * (m + 1)],
                                      waits=[("sd", sch.n["sd"])])
                        sdma(out[128 * c:128 * (c + 1), 512 * m:512 * (m + 1)],
                             h2pp[:, 0:512], waits=[("sd", ht)])
            if DEBUG_DUMP == "hbf":
                gdma(out[0:128, 0:NCH * BW], h1bnd[:, :],
                     waits=[(s, sch.n[s]) for s in ("dv", "ac", "pe")])
            allsems = [(s, sch.n[s]) for s in
                       ("sd", "se", "so", "hs0", "hs1")]
            gsems = [(s, sch.n[s]) for s in ("gd", "ge", "go", "gx")]
            sch.op("sync", None, waits=allsems)
            sch.op("gpsimd", None, waits=gsems + allsems)
            sch.flush(nc, sems)

        sch.verify()
    return nc


_NC_CACHE = None


def _get_nc():
    global _NC_CACHE
    if _NC_CACHE is None:
        _NC_CACHE = build_program()
    return _NC_CACHE


def kernel(L0_attn_x, L0_attn_W_attn, L0_attn_b_attn, L0_attn_W_proj,
           L0_attn_b_proj, L0_ffn_W_r, L0_ffn_W_k, L0_ffn_W_v, L0_ffn_mu_r,
           L0_ffn_mu_k):
    x = np.asarray(L0_attn_x, dtype=np.float32)
    wA = np.ascontiguousarray(np.asarray(L0_attn_W_attn, dtype=np.float32))
    bA = np.asarray(L0_attn_b_attn, dtype=np.float32)
    wP = np.ascontiguousarray(np.asarray(L0_attn_W_proj, dtype=np.float32))
    bP = np.asarray(L0_attn_b_proj, dtype=np.float32)
    wRT = np.ascontiguousarray(np.asarray(L0_ffn_W_r, dtype=np.float32).T)
    wKT = np.ascontiguousarray(np.asarray(L0_ffn_W_k, dtype=np.float32).T)
    wVT = np.ascontiguousarray(np.asarray(L0_ffn_W_v, dtype=np.float32).T)
    mu_r = np.asarray(L0_ffn_mu_r, dtype=np.float32)
    mu_k = np.asarray(L0_ffn_mu_k, dtype=np.float32)

    bqc = np.ascontiguousarray(bA[0:1024].reshape(8, 128).T)
    bkc = np.ascontiguousarray(bA[1024:2048].reshape(8, 128).T)
    bvc = np.ascontiguousarray(bA[2048:3072].reshape(8, 128).T)
    bpc = np.ascontiguousarray(bP.reshape(8, 128).T)
    murc = np.ascontiguousarray(mu_r.reshape(8, 128).T)
    mukc = np.ascontiguousarray(mu_k.reshape(8, 128).T)
    identd = np.eye(128, dtype=np.float32)

    pk = np.arange(128)[:, None]
    in_maps = []
    own_idx_by_core = []
    for core in range(8):
        b, p = divmod(core, 2)
        own = np.array([t for t in range(T) if (t % 128) // 64 == p])
        own_idx_by_core.append(own)
        xTf = np.ascontiguousarray(x[b].T)
        xTq = np.ascontiguousarray(xTf[:, own])
        bnd = 128 * np.arange(NS) + 64 * p - 1
        xTbv = np.zeros((C, NS), dtype=np.float32)
        valid = bnd >= 0
        xTbv[:, valid] = xTf[:, bnd[valid]]
        maskd_ = (pk <= np.arange(64)[None, :] + 64 * p).astype(np.float32)
        maskb_ = ((pk <= 63) if p == 1 else (pk <= -1)).astype(np.float32)
        bval = valid.astype(np.float32)[None, :]
        in_maps.append({
            "xT": xTf, "xTq": xTq, "xTb": np.ascontiguousarray(xTbv),
            "wA": wA, "wP": wP, "wRT": wRT, "wKT": wKT, "wVT": wVT,
            "bqc": bqc, "bkc": bkc, "bvc": bvc, "bpc": bpc,
            "murc": murc, "mukc": mukc,
            "maskd": np.ascontiguousarray(maskd_),
            "maskb": np.ascontiguousarray(maskb_),
            "bvalid": np.ascontiguousarray(bval),
            "identd": identd,
        })

    nc = _get_nc()
    res = run_bass_kernel_spmd(nc, in_maps, core_ids=list(range(8)))

    outp = np.zeros((4, T, C), dtype=np.float32)
    for core in range(8):
        b = core // 2
        outp[b, own_idx_by_core[core], :] = res.results[core]["out"].T
    return outp
